# version 15
# baseline (speedup 1.0000x reference)
import numpy as np
import concourse.bass as bass
import concourse.mybir as mybir
from concourse.bass_utils import run_bass_kernel_spmd

F32 = mybir.dt.float32
BF16 = mybir.dt.bfloat16
I32 = mybir.dt.int32
AF = mybir.ActivationFunctionType
ALU = mybir.AluOpType
AX = mybir.AxisListType

MAGIC = 0x5F3759DF
EPS = 1e-5


class Cfg:
    def __init__(self, D, DI, NE, IM, ISH, TC, TT, CT):
        self.D, self.DI, self.NE, self.IM, self.ISH = D, DI, NE, IM, ISH
        self.TC, self.TT, self.CT = TC, TT, CT
        self.KD = D // 128
        self.FXI = DI // 128
        self.FH = IM // 128
        self.FSH = ISH // 128
        self.NT = TC // TT
        self.PT = TT // 128
        self.CO = D // 128
        self.UPT = D // TT
        assert D % TT == 0
        self.n_D = self.FXI * 2 * self.KD
        self.n_E = self.CO * self.FXI
        self.n_I = self.FSH * 2 * self.KD
        self.n_H = NE * self.FH * 2 * self.KD
        self.n_J = self.CO * (NE * self.FH + self.FSH)
        self.n_pass = self.n_D + self.n_E + self.n_I + self.n_H + self.n_J
        self.n_chunks = -(-self.n_pass // CT)


REAL = Cfg(D=2048, DI=4096, NE=8, IM=1024, ISH=4096, TC=2048, TT=512, CT=24)
MINI = Cfg(D=256, DI=512, NE=4, IM=128, ISH=256, TC=256, TT=128, CT=6)
NSLOT = 3


def stream_tile_sources(cfg):
    KD, FXI, FH, FSH, NE, CO = cfg.KD, cfg.FXI, cfg.FH, cfg.FSH, cfg.NE, cfg.CO
    for f in range(FXI):
        for k in range(KD):
            yield ("w_inT", k * 128, f * 128)
        for k in range(KD):
            yield ("w_inT", k * 128, cfg.DI + f * 128)
    for c in range(CO):
        for k in range(FXI):
            yield ("w_outT", k * 128, c * 128)
    for f in range(FSH):
        for k in range(KD):
            yield ("w_sh_gate_up", k * 128, f * 128)
        for k in range(KD):
            yield ("w_sh_gate_up", k * 128, cfg.ISH + f * 128)
    for e in range(NE):
        for f in range(FH):
            for k in range(KD):
                yield ("w_gate_up_%d" % e, k * 128, f * 128)
            for k in range(KD):
                yield ("w_gate_up_%d" % e, k * 128, cfg.IM + f * 128)
    for c in range(CO):
        for e in range(NE):
            for k in range(FH):
                yield ("w_down_%d" % e, k * 128, c * 128)
        for k in range(FSH):
            yield ("w_sh_down", k * 128, c * 128)


def pack_wstream(cfg, mats):
    CT = cfg.CT
    out = np.zeros((cfg.n_chunks, 128, CT * 128), dtype=np.float32)
    for i, (name, r0, c0) in enumerate(stream_tile_sources(cfg)):
        ch, t = divmod(i, CT)
        out[ch, :, t * 128:(t + 1) * 128] = mats[name][r0:r0 + 128, c0:c0 + 128]
    return out


class Emitter:

    def __init__(self):
        self.ops = {"pe": [], "ve": [], "ae": [], "po": []}
        self.cnt = {"pe": 0, "ve": 0, "ae": 0, "po": 6}
        self.lastw = {}
        self.lastr = {}
        self.hi = {"pe": {}, "ve": {}, "ae": {}, "po": {}, "sy": {}}

    def preset_write(self, key, sem_name, value):
        self.lastw[key] = (sem_name, value)
        self.lastr[key] = []

    def note_read(self, key, sem_name, value):
        self.lastr.setdefault(key, []).append((sem_name, value))

    def waits_for(self, eng, reads, writes, wars=()):
        need = {}
        for k in reads:
            if k in self.lastw:
                s, v = self.lastw[k]
                need[s] = max(need.get(s, 0), v)
        for k in wars:
            for s, v in self.lastr.get(k, []):
                need[s] = max(need.get(s, 0), v)
            if k in self.lastw:
                s, v = self.lastw[k]
                need[s] = max(need.get(s, 0), v)
        for k in writes:
            if k in self.lastw:
                s, v = self.lastw[k]
                need[s] = max(need.get(s, 0), v)
            for s, v in self.lastr.get(k, []):
                need[s] = max(need.get(s, 0), v)
        out = []
        for s, v in sorted(need.items()):
            if self.hi[eng].get(s, 0) < v:
                out.append((s, v))
                self.hi[eng][s] = v
        return out

    def op(self, eng, fn, reads=(), writes=(), n_incs=1, wars_extra=()):
        waits = self.waits_for(eng, reads, writes, wars_extra)
        self.cnt[eng] += n_incs
        mark = self.cnt[eng]
        self.ops[eng].append((waits, fn))
        for k in writes:
            self.lastw[k] = (eng, mark)
            self.lastr[k] = []
        for k in reads:
            self.note_read(k, eng, mark)
        return mark


def build_program(cfg):
    D, DI, NE, IM, ISH = cfg.D, cfg.DI, cfg.NE, cfg.IM, cfg.ISH
    TC, TT, CT = cfg.TC, cfg.TT, cfg.CT
    KD, FXI, FH, FSH, NT, PT, CO, UPT = (cfg.KD, cfg.FXI, cfg.FH, cfg.FSH,
                                         cfg.NT, cfg.PT, cfg.CO, cfg.UPT)
    NCH = cfg.n_chunks
    total_chunks = NCH * NT
    NU = NE * FH + FSH
    BIGC = max(NU * TT, FXI * TT, (2 * PT + 1) * D + 2 * D)

    nc = bass.Bass("TRN2", target_bir_lowering=False)
    x_h = nc.dram_tensor("x", [TC + 3, D], F32, kind="ExternalInput").ap()
    ws_h = nc.dram_tensor("ws", [NCH, 128, CT * 128], F32, kind="ExternalInput").ap()
    cw_h = nc.dram_tensor("cw", [128, FXI * 4], F32, kind="ExternalInput").ap()
    cb_h = nc.dram_tensor("cb", [128, FXI], F32, kind="ExternalInput").ap()
    dd_h = nc.dram_tensor("dd", [128, FXI], F32, kind="ExternalInput").ap()
    wg_h = nc.dram_tensor("wg", [128, KD * NE], F32, kind="ExternalInput").ap()
    out_h = nc.dram_tensor("out", [TC, D], F32, kind="ExternalOutput").ap()
    wsb_h = nc.dram_tensor("wsb", [NCH, 128, CT * 128], BF16).ap()

    from contextlib import ExitStack
    with ExitStack() as ctx:
        en = ctx.enter_context
        xt = [en(nc.sbuf_tensor("xt%d" % i, [128, PT * D], F32)) for i in range(2)]
        xlt = en(nc.sbuf_tensor("xlt", [128, KD * TT], BF16))
        xlh = en(nc.sbuf_tensor("xlh", [128, KD * 3], BF16))
        big = en(nc.sbuf_tensor("big", [128, BIGC], BF16))
        xib = en(nc.sbuf_tensor("xib", [128, 2 * (TT + 4)], BF16))
        tail = en(nc.sbuf_tensor("tail", [128, FXI * 3], BF16))
        wbuf = en(nc.sbuf_tensor("wbuf", [128, NSLOT * CT * 128], BF16))
        diag = en(nc.sbuf_tensor("diag", [128, 4 * 128], BF16))
        sgb = en(nc.sbuf_tensor("sgb", [128, 2 * TT], BF16))
        ev = en(nc.sbuf_tensor("ev", [128, 2 * TT], BF16))
        ident_b = en(nc.sbuf_tensor("ident_b", [128, 128], BF16))
        ident_f = en(nc.sbuf_tensor("ident_f", [128, 128], F32))
        ones1 = en(nc.sbuf_tensor("ones1", [1, 128], BF16))
        cw = en(nc.sbuf_tensor("cw_s", [128, FXI * 4], F32))
        cb = en(nc.sbuf_tensor("cb_s", [128, FXI], F32))
        dd = en(nc.sbuf_tensor("dd_s", [128, FXI], F32))
        wg = en(nc.sbuf_tensor("wg_s", [128, KD * NE], BF16))
        stat = en(nc.sbuf_tensor("stat", [128, 32], F32))
        stat2 = en(nc.sbuf_tensor("stat2", [128, 2 * 4 * 16], F32))
        stati = en(nc.sbuf_tensor("stati", [128, 4], I32))
        statih = en(nc.sbuf_tensor("statih", [3, 1], I32))
        icon = en(nc.sbuf_tensor("icon", [128, 4], I32))
        rtm = en(nc.sbuf_tensor("rtm", [128, PT * NE], F32))
        ctm = en(nc.sbuf_tensor("ctm", [128, PT * NE], F32))
        rt1 = en(nc.sbuf_tensor("rt1", [128, PT * NE], F32))
        rt2 = en(nc.sbuf_tensor("rt2", [128, PT * NE], F32))
        rsg = en(nc.sbuf_tensor("rsg", [128, PT * NE], F32))
        rst = en(nc.sbuf_tensor("rst", [128, 16], F32))
        cem = en(nc.sbuf_tensor("cem", [NE, TT], BF16))
        crow = en(nc.sbuf_tensor("crow", [1, TT], BF16))
        bce = en(nc.sbuf_tensor("bce", [128, TT], BF16))
        p = [en(nc.psum_tensor("p%d" % i, [128, 512], F32)) for i in range(6)]
        pb = [en(nc.psum_tensor("pb%d" % i, [128, 1024], BF16)) for i in range(2)]
        sems = {}
        for s in ["pe", "ve", "ae", "po", "sd0", "sd1", "wh0", "wh1",
                  "wh2", "wb0", "wb1", "wb2", "wgd", "xd", "cs", "hl", "cr0", "cr1", "st"]:
            sems[s] = en(nc.semaphore(s))
        block = en(nc.Block())

        if PT * D >= 2 * CT * 128:
            stg_dedicated = None
        else:
            stg_dedicated = en(nc.sbuf_tensor("stg", [128, 2 * CT * 128], F32))

        def stg_ap(slot, b=0):
            if stg_dedicated is not None:
                return stg_dedicated[:, slot * CT * 128:(slot + 1) * CT * 128]
            return xt[1 - (b // NCH)][:, slot * CT * 128:(slot + 1) * CT * 128]

        em = Emitter()
        sync_stream = []
        sy_dma = [0]
        st_dma = [0]

        sy_cnt = {}

        def sy_op(sem, fn, reads=(), writes=(), wars=(), serialize=False,
                  lazy_wars=(), writes_clean=()):
            waits = em.waits_for("sy", reads, writes, wars)
            sy_cnt[sem] = sy_cnt.get(sem, 0) + 1
            mark = 16 * sy_cnt[sem]
            if sem == "st":
                st_dma[0] += 1
            waits = [(s, v) for s, v in waits if not (s == sem and v >= mark)]
            sync_stream.append((waits, fn, sem, mark if serialize else None,
                                tuple(lazy_wars)))
            for k in writes_clean:
                em.lastw[k] = (sem, mark)
            for k in writes:
                em.lastw[k] = (sem, mark)
                em.lastr[k] = []
            for k in reads:
                em.note_read(k, sem, mark)
            return mark


        def pe_op(fn, reads=(), writes=()):
            return em.op("pe", fn, reads=reads, writes=writes, n_incs=1)

        def ac_op(sem, fn, reads=(), writes=(), wars=(), serialize=False):
            waits = em.waits_for("ae", reads, writes, wars)
            sy_cnt[sem] = sy_cnt.get(sem, 0) + 1
            mark = 16 * sy_cnt[sem]
            if sem == "st":
                st_dma[0] += 1
            waits = [(s, v) for s, v in waits if not (s == sem and v >= mark)]

            def wrapped(a, fn=fn, sem=sem, waits=waits, mark=mark,
                        serialize=serialize):
                ins = fn(a)
                ins.then_inc(sems[sem], 16)
                if serialize:
                    a.wait_ge(sems[sem], mark)
                return None
            em.ops["ae"].append((waits, wrapped, True))
            for k in writes:
                em.lastw[k] = (sem, mark)
                em.lastr[k] = []
            for k in reads:
                em.note_read(k, sem, mark)
            return mark

        def ve_op(fn, reads=(), writes=(), n=1):
            return em.op("ve", fn, reads=reads, writes=writes, n_incs=n)

        def ae_op(fn, reads=(), writes=()):
            return em.op("ae", fn, reads=reads, writes=writes, n_incs=1)

        em.preset_write("ident_b", "po", 2)
        em.preset_write("ident_f", "po", 4)
        em.preset_write("ones1", "po", 5)
        em.preset_write("icon", "po", 6)

        _whcnt = {}
        for a in range(total_chunks):
            if a >= NCH and not (a < 2 * NCH and (a % NCH) >= (NCH // 3 if NT > 1 else NCH)):
                sname = "wh%d" % (a % NSLOT)
                _whcnt[sname] = _whcnt.get(sname, 0) + 1
                em.preset_write(("wch", a), sname, 16 * _whcnt[sname])


        wg_stg = ev[:, 0:2 * KD * NE].bitcast(F32)
        ac_op("cs", lambda s: s.dma_start(wg_stg, wg_h),
              writes=[("ev", 0), ("ev", 1), "wg_stg"])
        ve_op(lambda v: v.tensor_copy(wg[:], wg_stg),
              reads=["wg_stg"], writes=["wg"])
        ac_op("cr0", lambda s: s.dma_start(cw[:], cw_h), writes=["cw"])
        ac_op("cr1", lambda s: s.dma_start(cb[:], cb_h), writes=["cb"])
        ac_op("wgd", lambda s: s.dma_start(dd[:], dd_h), writes=["dd"])

        def mk_xload(buf_i, tau):
            def f(s, buf_i=buf_i, tau=tau):
                dst = xt[buf_i][:].rearrange("p (t d) -> p t d", t=PT)
                src = x_h[3 + tau * TT: 3 + (tau + 1) * TT, :].rearrange(
                    "(t p) d -> p t d", p=128)
                return s.dma_start(dst, src)
            return f
        ac_op("xd", mk_xload(0, 0), writes=[("xt", 0)], serialize=True)
        HB0 = (PT + 1 + PT) * D
        halo_v = big[0:3, HB0: HB0 + 2 * D].bitcast(F32)
        hrkeys = [("big", (HB0 + u * TT) // TT) for u in range(2 * UPT)]
        ac_op("hl", lambda s: s.dma_start(halo_v, x_h[0:3, :]),
              writes=["halo"] + hrkeys)

        wt = [0]
        claimed = set()
        wb_cnt = [0]
        pend_loads = []
        pend_stage = []
        pend0_seen = set()

        def flush_stage():
            while pend_stage and wt[0] >= (pend_stage[0] - 2) * CT:
                emit_stage_cast(pend_stage.pop(0))

        def flush_wloads():
            flush_stage()
            while pend_loads and wt[0] >= (pend_loads[0] - NSLOT + 1) * CT:
                b = pend_loads.pop(0)
                slot = b % NSLOT

                def ld(s, b=b, slot=slot):
                    return s.dma_start(
                        wbuf[:, slot * CT * 128:(slot + 1) * CT * 128],
                        wsb_h[b % NCH])
                sy_op("wh%d" % (b % NSLOT), ld,
                      reads=[("wsb", b % NCH)], wars=[("wch", b - NSLOT)],
                      writes_clean=[("wch", b)])

        WBCUT = NCH // 3 if NT > 1 else NCH

        def emit_stage_cast(b):
            sslot = b % 2
            wslot = b % NSLOT

            def ld(s, b=b, sslot=sslot):
                return s.dma_start(stg_ap(sslot, b), ws_h[b % NCH])
            extra_wars = [("stg", b - 2)]
            if b == NCH + WBCUT:
                extra_wars = [("xt", 0)]
            sy_op("sd%d" % sslot, ld, wars=extra_wars,
                  writes_clean=[("stg", b)])
            eng = "ve" if (b % 3) < 2 else "ae"

            def cast(h, b=b, sslot=sslot, wslot=wslot, eng=eng):
                dst = wbuf[:, wslot * CT * 128:(wslot + 1) * CT * 128]
                if eng == "ae":
                    return h.copy(dst, stg_ap(sslot, b))
                return h.tensor_copy(dst, stg_ap(sslot, b))
            em.op(eng, cast, reads=[("stg", b)],
                  writes=[("wch", b)], wars_extra=[("wch", b - NSLOT)])

        def is_staged(a):
            return a < NCH or (a < 2 * NCH and (a % NCH) >= WBCUT)

        def needs_wb(a):
            return (a < NCH and (a % NCH) < WBCUT) or                    (NCH <= a < 2 * NCH and (a % NCH) >= WBCUT)

        def on_first_claim(a):
            if needs_wb(a):
                slot = a % NSLOT
                wb_cnt[0] += 1

                def wbk(s, a=a, slot=slot):
                    return s.dma_start(
                        wsb_h[a % NCH],
                        wbuf[:, slot * CT * 128:(slot + 1) * CT * 128])
                sy_op("wb%d" % (a % NSLOT), wbk, reads=[("wch", a)],
                      writes=[("wsb", a % NCH)])
            for b in (a + 2, a + NSLOT):
                if 2 <= b < total_chunks and is_staged(b) and b not in pend0_seen:
                    pend0_seen.add(b)
                    pend_stage.append(b)
            b = a + NSLOT
            if b < total_chunks and not is_staged(b):
                pend_loads.append(b)

        def wtiles(n):
            t0 = wt[0]
            wt[0] += n
            out = []
            for t in range(t0, t0 + n):
                a = t // CT
                if a not in claimed:
                    claimed.add(a)
                    on_first_claim(a)
                out.append((wbuf[:, (t % (NSLOT * CT)) * 128:
                                 ((t % (NSLOT * CT)) + 1) * 128], a))
            return out

        def mm_group(psum_ap, rhs_list, tiles, extra_reads, wkey,
                     no_flush=False, grp_start=True, grp_stop=True):
            n = len(tiles)
            parts = []
            cur = [0]
            for i in range(1, n):
                if tiles[i][1] != tiles[i - 1][1]:
                    parts.append(cur)
                    cur = []
                cur.append(i)
            parts.append(cur)
            first_extra = list(extra_reads)
            for pi, idxs in enumerate(parts):
                chunks = sorted({tiles[i][1] for i in idxs})
                ckeys = [("wch", c) for c in chunks]
                is_first = pi == 0
                is_last = pi == len(parts) - 1

                def f(t, idxs=idxs, is_first=is_first, is_last=is_last,
                      psum_ap=psum_ap, rhs_list=rhs_list, tiles=tiles, n=n,
                      grp_start=grp_start, grp_stop=grp_stop):
                    ins = None
                    for j, i in enumerate(idxs):
                        ins = t.matmul(psum_ap, tiles[i][0], rhs_list[i],
                                       start=(grp_start and is_first and j == 0),
                                       stop=(grp_stop and is_last
                                             and j == len(idxs) - 1))
                    return ins
                pe_op(f, reads=ckeys + (first_extra if is_first else []),
                      writes=[wkey])
                first_extra = []
            if not no_flush:
                flush_wloads()

        def layernorm(src_fn, n_pt, sbase, dump_u0, key_src, key_out,
                      int_sc, part=128, from_partials=False):
            sb = sbase
            dump_col = dump_u0 * TT
            upp = D // TT
            if from_partials:
                for pt in range(n_pt):
                    pk = ([("s1p", pt, c) for c in range(CO)]
                          + [("s2p", pt, c) for c in range(CO)])
                    ve_op(lambda v, pt=pt: v.reduce_sum(
                        stat[:, sb + pt: sb + pt + 1],
                        stat2[:, pt * CO:(pt + 1) * CO], axis=AX.X),
                        reads=pk, writes=[("st1", sb, pt)])
                    ve_op(lambda v, pt=pt: v.reduce_sum(
                        stat[:, sb + n_pt + pt: sb + n_pt + pt + 1],
                        stat2[:, PT * CO + pt * CO: PT * CO + (pt + 1) * CO],
                        axis=AX.X),
                        reads=pk, writes=[("st2", sb, pt)])
            else:
                for pt in range(n_pt):
                    ukeys = [("big", dump_u0 + pt * upp + u) for u in range(upp)]
                    ae_op(lambda a, pt=pt: a.activation(
                        big[0:part, dump_col + pt * D: dump_col + (pt + 1) * D],
                        src_fn(pt), AF.Square,
                        accum_out=stat[0:part, sb + n_pt + pt: sb + n_pt + pt + 1]),
                        reads=[key_src], writes=ukeys + [("st2", sb, pt)])
                    sdc = (PT + 1) * D + pt * D
                    sukeys = [("big", (sdc // TT) + u) for u in range(upp)]
                    ve_op(lambda v, pt=pt, sdc=sdc: v.tensor_scalar(
                        big[0:part, sdc: sdc + D],
                        src_fn(pt), 1.0, None, ALU.mult, ALU.add,
                        accum_out=stat[0:part, sb + pt: sb + pt + 1]),
                        reads=[key_src],
                        writes=[("st1", sb, pt)] + sukeys)
            def fin_apply(p0, p1):
                npg = p1 - p0
                s1 = stat[0:part, sb + p0: sb + p1]
                s2 = stat[0:part, sb + n_pt + p0: sb + n_pt + p1]
                mean = stat[0:part, sb + 2 * n_pt + p0: sb + 2 * n_pt + p1]
                var = stat[0:part, sb + 3 * n_pt + p0: sb + 3 * n_pt + p1]
                skeys = ([("st1", sb, pt) for pt in range(p0, p1)]
                         + [("st2", sb, pt) for pt in range(p0, p1)])
                fk = [("fin", sb, p0)]
                isc = int_sc[0:part, p0:p1]
                iconp = icon[0:part, p0:p1]
                steps = [
                    lambda v: v.tensor_scalar(mean, s1, 1.0 / D, None, ALU.mult),
                    lambda v: v.tensor_scalar(var, s2, 1.0 / D, EPS, ALU.mult,
                                              ALU.add),
                    lambda v: v.tensor_tensor(s2, mean, mean, ALU.mult),
                    lambda v: v.tensor_tensor(var, var, s2, ALU.subtract),
                    lambda v: v.tensor_scalar(isc, var.bitcast(I32), 1, None,
                                              ALU.arith_shift_right),
                    lambda v: v.tensor_tensor(isc, iconp, isc, ALU.subtract),
                    lambda v: v.tensor_copy(s1, isc.bitcast(F32)),
                ]
                for _ in range(2):
                    steps += [
                        lambda v: v.tensor_tensor(s2, s1, s1, ALU.mult),
                        lambda v: v.tensor_tensor(s2, s2, var, ALU.mult),
                        lambda v: v.tensor_scalar(s2, s2, -0.5, 1.5, ALU.mult,
                                                  ALU.add),
                        lambda v: v.tensor_tensor(s1, s1, s2, ALU.mult),
                    ]
                for st_ in steps:
                    ve_op(st_, reads=skeys + fk + ["icon"], writes=fk)
                inv = s1
                for pt in range(p0, p1):
                    ukeys = [("big", dump_u0 + pt * upp + u) for u in range(upp)]
                    ve_op(lambda v, pt=pt: v.tensor_scalar(
                        big[0:part, dump_col + pt * D: dump_col + (pt + 1) * D],
                        src_fn(pt),
                        mean[0:part, pt - p0: pt - p0 + 1],
                        inv[0:part, pt - p0: pt - p0 + 1],
                        ALU.subtract, ALU.mult),
                        reads=fk + [key_src], writes=ukeys + [(key_out, pt)])
            fin_apply(0, n_pt)

        def transpose_to_xlt(key_out):
            nb = [0]
            upp = D // TT
            for pt in range(PT):
                for kb0 in range(0, KD, 8):
                    kn = min(8, KD - kb0)
                    bank = nb[0] % 2
                    nb[0] += 1
                    ukeys = [("big", pt * upp + u) for u in range(upp)]

                    def f(t, pt=pt, kb0=kb0, kn=kn, bank=bank):
                        ins = None
                        for i in range(kn):
                            ins = t.transpose(
                                pb[bank][:, i * 128:(i + 1) * 128],
                                big[:, pt * D + (kb0 + i) * 128:
                                    pt * D + (kb0 + i + 1) * 128],
                                ident_b[:])
                        return ins
                    pe_op(f, reads=ukeys + [("ln", pt), "ident_b"],
                          writes=[("pb", bank)])

                    def g(v, pt=pt, kb0=kb0, kn=kn, bank=bank):
                        dst = xlt[:].rearrange("p (k t) -> p k t", k=KD)[
                            :, kb0:kb0 + kn, pt * 128:(pt + 1) * 128]
                        src = pb[bank][:, 0:kn * 128].rearrange(
                            "p (k m) -> p k m", k=kn)
                        return v.tensor_copy(dst, src)
                    ve_op(g, reads=[("pb", bank)], writes=["xlt"])

        def halo_ln():
            src = lambda pt: halo_v
            layernorm(src, 1, 24, PT * UPT, "halo", "hln", statih, part=3)
            hu = [("big", PT * UPT + u) for u in range(UPT)]

            def f(t):
                ins = None
                for k in range(KD):
                    ins = t.transpose(
                        pb[0][0:128, k * 4: k * 4 + 3],
                        big[0:3, PT * D + k * 128: PT * D + (k + 1) * 128],
                        ident_b[0:3, 0:3])
                return ins
            pe_op(f, reads=hu + [("hln", 0), "ident_b"], writes=[("pb", 0)])

            def g(v):
                dst = xlh[:].rearrange("p (k c) -> p k c", k=KD)
                srcv = pb[0][:, 0:KD * 4].rearrange(
                    "p (k c) -> p k c", k=KD)[:, :, 0:3]
                return v.tensor_copy(dst, srcv)
            ve_op(g, reads=[("pb", 0)], writes=["xlh"])

        def attn_up(tau):
            for f in range(FXI):
                xi_bank = 0 if tau == 0 else (f % 2)
                g_bank = 2 + (f % 2)
                xc_bank = 4 + (f % 2)
                xb = f % 2
                sga = f % 2
                rhs = [xlt[:].rearrange("p (k t) -> p k t", k=KD)[:, k, :]
                       for k in range(KD)]
                xi_tiles = wtiles(KD)
                mm_group(p[xi_bank][:, 0:TT], rhs, xi_tiles,
                         ["xlt"], ("p", xi_bank), no_flush=(tau == 0))
                if tau == 0:
                    rhs_h = [xlh[:, k * 3:(k + 1) * 3] for k in range(KD)]
                    mm_group(p[1][:, 0:3], rhs_h, xi_tiles, ["xlh"], ("p", 1))
                    ve_op(lambda v, f=f: v.tensor_copy(
                        tail[:, f * 3:(f + 1) * 3], p[1][:, 0:3]),
                        reads=[("p", 1)], writes=[("tail", f)])
                g_tiles = wtiles(KD)
                mm_group(p[g_bank][:, 0:TT], rhs, g_tiles, ["xlt"], ("p", g_bank))
                ve_op(lambda v, f=f, xb=xb: v.tensor_copy(
                    xib[:, xb * (TT + 4): xb * (TT + 4) + 3],
                    tail[:, f * 3:(f + 1) * 3]),
                    reads=[("tail", f)], writes=[("xib", xb)])
                ve_op(lambda v, xb=xb, xi_bank=xi_bank: v.tensor_copy(
                    xib[:, xb * (TT + 4) + 3: xb * (TT + 4) + 3 + TT],
                    p[xi_bank][:, 0:TT]),
                    reads=[("p", xi_bank)], writes=[("xib", xb)])
                ve_op(lambda v, f=f, xb=xb: v.tensor_copy(
                    tail[:, f * 3:(f + 1) * 3],
                    xib[:, xb * (TT + 4) + TT: xb * (TT + 4) + TT + 3]),
                    reads=[("xib", xb)], writes=[("tail", f)])

                def mkdiag(v, f=f):
                    ins = None
                    for j in range(4):
                        if ins is not None:
                            ins.then_inc(sems["ve"], 1)
                        ins = v.tensor_scalar(
                            diag[:, j * 128:(j + 1) * 128],
                            ident_b[:], cw[:, f * 4 + j: f * 4 + j + 1], None,
                            ALU.mult)
                    return ins
                ve_op(mkdiag, reads=["ident_b", "cw"], writes=["diag"], n=4)

                def conv(t, xb=xb, xc_bank=xc_bank):
                    ins = None
                    for j in range(4):
                        ins = t.matmul(
                            p[xc_bank][:, 0:TT],
                            diag[:, j * 128:(j + 1) * 128],
                            xib[:, xb * (TT + 4) + j: xb * (TT + 4) + j + TT],
                            start=(j == 0), stop=(j == 3))
                    return ins
                pe_op(conv, reads=["diag", ("xib", xb)],
                      writes=[("p", xc_bank)])
                ae_op(lambda a, g_bank=g_bank, sga=sga: a.activation(
                    sgb[:, sga * TT:(sga + 1) * TT], p[g_bank][:, 0:TT], AF.Silu),
                    reads=[("p", g_bank)], writes=[("sgb", sga)])
                ae_op(lambda a, xc_bank=xc_bank, sga=sga, f=f: a.activation(
                    ev[:, sga * TT:(sga + 1) * TT], p[xc_bank][:, 0:TT], AF.Silu,
                    bias=cb[:, f: f + 1]),
                    reads=[("p", xc_bank), "cb"], writes=[("ev", sga)])
                ve_op(lambda v, f=f, sga=sga: v.scalar_tensor_tensor(
                    big[:, f * TT:(f + 1) * TT],
                    ev[:, sga * TT:(sga + 1) * TT],
                    dd[:, f: f + 1],
                    sgb[:, sga * TT:(sga + 1) * TT],
                    ALU.mult, ALU.mult),
                    reads=[("ev", sga), ("sgb", sga), "dd"],
                    writes=[("big", f)])

        def residual_block(c, bank, cur, ln_stats=False):
            eb = bank
            ae_op(lambda a, bank=bank, eb=eb: a.copy(
                ev[:, eb * TT:(eb + 1) * TT], p[bank][:, 0:TT]),
                reads=[("p", bank)], writes=[("ev", eb)])

            def tr(t, eb=eb, bank=bank):
                ins = None
                for pt in range(PT):
                    ins = t.transpose(
                        pb[bank][:, pt * 128:(pt + 1) * 128],
                        ev[:, eb * TT + pt * 128: eb * TT + (pt + 1) * 128],
                        ident_b[:])
                return ins
            pe_op(tr, reads=[("ev", eb), "ident_b"], writes=[("pb", bank)])

            def add(v, c=c, bank=bank, cur=cur):
                dst = xt[cur][:].rearrange("p (t d) -> p t d", t=PT)[
                    :, :, c * 128:(c + 1) * 128]
                src = pb[bank][:, 0:PT * 128].rearrange("p (t m) -> p t m", t=PT)
                return v.tensor_tensor(dst, dst, src, ALU.add)
            ve_op(add, reads=[("pb", bank), ("xt", cur)], writes=[("xt", cur)])
            if ln_stats:
                for pt in range(PT):
                    xs_ = xt[cur][:, pt * D + c * 128: pt * D + (c + 1) * 128]
                    ae_op(lambda a, xs_=xs_, pt=pt, c=c: a.activation(
                        p[2][:, pt * 128:(pt + 1) * 128],
                        xs_, AF.Square,
                        accum_out=stat2[:, PT * CO + pt * CO + c:
                                        PT * CO + pt * CO + c + 1]),
                        reads=[("xt", cur)],
                        writes=[("s2p", pt, c), ("p", 2)]),
                    ve_op(lambda v, xs_=xs_, pt=pt, c=c: v.tensor_scalar(
                        p[3][:, pt * 128:(pt + 1) * 128],
                        xs_, 1.0, None, ALU.mult, ALU.add,
                        accum_out=stat2[:, pt * CO + c: pt * CO + c + 1]),
                        reads=[("xt", cur)],
                        writes=[("s1p", pt, c), ("p", 3)])

        def attn_down(tau, cur):
            hkeys = [("big", f) for f in range(FXI)]
            prev = None
            for c in range(CO):
                bank = c % 2
                tiles = wtiles(FXI)
                rhs = [big[:, k * TT:(k + 1) * TT] for k in range(FXI)]
                mm_group(p[bank][:, 0:TT], rhs, tiles, hkeys, ("p", bank))
                if prev is not None:
                    residual_block(*prev, ln_stats=True)
                prev = (c, bank, cur)
            residual_block(*prev, ln_stats=True)

        def router(tau):
            def f(t):
                ins = None
                for k in range(KD):
                    ins = t.matmul(p[0][0:NE, 0:TT],
                                   wg[:, k * NE:(k + 1) * NE],
                                   xlt[:].rearrange("p (k t) -> p k t", k=KD)[:, k, :],
                                   start=(k == 0), stop=(k == KD - 1))
                return ins
            pe_op(f, reads=["wg", "xlt"], writes=[("p", 0)])
            lre = ev[:].bitcast(F32)
            ve_op(lambda v: v.tensor_copy(lre[0:NE, 0:TT], p[0][0:NE, 0:TT]),
                  reads=[("p", 0)], writes=[("ev", 0), ("ev", 1)])

            def trl(t):
                ins = None
                for pt in range(PT):
                    ins = t.transpose(p[2][0:128, pt * NE:(pt + 1) * NE],
                                      lre[0:NE, pt * 128:(pt + 1) * 128],
                                      ident_f[0:NE, 0:NE])
                return ins
            pe_op(trl, reads=[("ev", 0), ("ev", 1), "ident_f"],
                  writes=[("p", 2)])
            ve_op(lambda v: v.tensor_copy(rtm[:, 0:PT * NE], p[2][:, 0:PT * NE]),
                  reads=[("p", 2)], writes=["rtm"])
            l3 = rtm[:].rearrange("p (g e) -> p g e", g=PT)
            m1 = rst[:, 0:PT]
            m2 = rst[:, PT:2 * PT]
            sb_ = rst[:, 2 * PT:3 * PT]
            m1b = m1.unsqueeze(2).broadcast_to([128, PT, NE])
            m2b = m2.unsqueeze(2).broadcast_to([128, PT, NE])
            ssb = sb_.unsqueeze(2).broadcast_to([128, PT, NE])
            rt1v = rt1[:].rearrange("p (g e) -> p g e", g=PT)
            rt2v = rt2[:].rearrange("p (g e) -> p g e", g=PT)
            rsgv = rsg[:].rearrange("p (g e) -> p g e", g=PT)
            ctmv = ctm[:].rearrange("p (g e) -> p g e", g=PT)
            ve_op(lambda v: v.reduce_max(m1, l3, axis=AX.X),
                  reads=["rtm"], writes=["rst"])
            ve_op(lambda v: v.tensor_tensor(rt1v, l3, m1b, ALU.is_ge),
                  reads=["rtm", "rst"], writes=["rt1"])
            ve_op(lambda v: v.scalar_tensor_tensor(
                rt1v, rt1v, -1e30, l3, ALU.mult, ALU.add),
                reads=["rt1", "rtm"], writes=["rt1"])
            ve_op(lambda v: v.reduce_max(m2, rt1v, axis=AX.X),
                  reads=["rt1"], writes=["rst"])
            ve_op(lambda v: v.tensor_tensor(rt2v, l3, m2b, ALU.is_ge),
                  reads=["rtm", "rst"], writes=["rt2"])
            ve_op(lambda v: v.tensor_tensor(sb_, m1, m2, ALU.add),
                  reads=["rst"], writes=["rst"])
            ve_op(lambda v: v.tensor_scalar(rsgv, l3, 2.0, None, ALU.mult),
                  reads=["rtm"], writes=["rsg"])
            ve_op(lambda v: v.tensor_tensor(rsgv, rsgv, ssb, ALU.subtract),
                  reads=["rsg", "rst"], writes=["rsg"])
            ae_op(lambda a: a.activation(rsg[:], rsg[:], AF.Sigmoid),
                  reads=["rsg"], writes=["rsg"])
            ve_op(lambda v: v.tensor_tensor(ctmv, rsgv, rt2v, ALU.mult),
                  reads=["rsg", "rt2"],
                  writes=[("ctm", pt) for pt in range(PT)])

            def trc(t):
                ins = None
                for pt in range(PT):
                    ins = t.transpose(p[3][0:NE, pt * 128:(pt + 1) * 128],
                                      ctm[:, pt * NE:(pt + 1) * NE],
                                      ident_f[:])
                return ins
            pe_op(trc, reads=[("ctm", pt) for pt in range(PT)] + ["ident_f"],
                  writes=[("p", 3)])
            ve_op(lambda v: v.tensor_copy(cem[:, 0:TT], p[3][0:NE, 0:TT]),
                  reads=[("p", 3)], writes=["cem"])
            pass

        def up_pair(f_rhs, g_bank, u_bank, sga, out_unit, with_bce, bslot=0):
            rhs = [xlt[:].rearrange("p (k t) -> p k t", k=KD)[:, k, :]
                   for k in range(KD)]
            g_tiles = wtiles(KD)
            mm_group(p[g_bank][:, 0:TT], rhs, g_tiles, ["xlt"], ("p", g_bank))
            u_tiles = wtiles(KD)
            mm_group(p[u_bank][:, 0:TT], rhs, u_tiles, ["xlt"], ("p", u_bank))
            ae_op(lambda a: a.activation(
                sgb[:, sga * TT:(sga + 1) * TT], p[g_bank][:, 0:TT], AF.Silu),
                reads=[("p", g_bank)], writes=[("sgb", sga)])
            if with_bce:
                ve_op(lambda v: v.tensor_tensor(
                    ev[:, sga * TT:(sga + 1) * TT],
                    sgb[:, sga * TT:(sga + 1) * TT],
                    p[u_bank][:, 0:TT], ALU.mult),
                    reads=[("sgb", sga), ("p", u_bank)], writes=[("ev", sga)])
                ve_op(lambda v: v.tensor_tensor(
                    big[:, out_unit * TT:(out_unit + 1) * TT],
                    ev[:, sga * TT:(sga + 1) * TT],
                    bce[:, 0:TT], ALU.mult),
                    reads=[("ev", sga), "bce"],
                    writes=[("big", out_unit)])
            else:
                ve_op(lambda v: v.tensor_tensor(
                    big[:, out_unit * TT:(out_unit + 1) * TT],
                    sgb[:, sga * TT:(sga + 1) * TT],
                    p[u_bank][:, 0:TT], ALU.mult),
                    reads=[("sgb", sga), ("p", u_bank)],
                    writes=[("big", out_unit)])

        def shared_pre(tau):
            base = NE * FH
            for f in range(0, min(2, FSH)):
                up_pair(f, f % 2, 2 + (f % 2), f % 2, base + f, False)

        def shared_post(tau):
            base = NE * FH
            for f in range(min(2, FSH), FSH):
                up_pair(f, f % 2, 2 + (f % 2), f % 2, base + f, False)

        def crow_dma(ee):
            ac_op("cr0", lambda s, ee=ee: s.dma_start(
                crow[0:1, 0:TT], cem[ee:ee + 1, 0:TT]),
                reads=["cem"], writes=["crow"])

        def bce_bcast(ee):
            cslot = ee % 2
            pe_op(lambda t, cslot=cslot: t.matmul(
                p[4 + cslot][:, 0:TT], ones1[:], crow[0:1, 0:TT],
                start=True, stop=True),
                reads=["crow", "ones1"], writes=[("p", 4 + cslot)])

        def bce_make(ee):
            crow_dma(ee)
            bce_bcast(ee)

        def bce_evac(ee):
            cslot = ee % 2
            ve_op(lambda v, cslot=cslot: v.tensor_copy(
                bce[:, 0:TT], p[4 + cslot][:, 0:TT]),
                reads=[("p", 4 + cslot)], writes=["bce"])

        def experts_up(tau):
            bce_make(0)
            for ee in range(NE):
                bce_evac(ee)
                for f in range(FH):
                    up_pair(f, f % 2, 2 + (f % 2), f % 2, ee * FH + f, True,
                            bslot=ee % 2)
                    if f == 0 and ee + 1 < NE:
                        crow_dma(ee + 1)
                    if f == min(2, FH - 1) and ee + 1 < NE:
                        bce_bcast(ee + 1)

        def ffn_down(tau, cur):
            akeys = [("big", i) for i in range(NU)]
            NJ = NE * FH + FSH
            PIECE = 2 * CT
            prevj = [None]
            for c in range(CO):
                bank = c % 2
                rhs = []
                for ee in range(NE):
                    for k in range(FH):
                        rhs.append(big[:, (ee * FH + k) * TT:
                                       (ee * FH + k + 1) * TT])
                for k in range(FSH):
                    rhs.append(big[:, (NE * FH + k) * TT:(NE * FH + k + 1) * TT])
                i0 = 0
                first = True
                while i0 < NJ:
                    nn_ = min(PIECE, NJ - i0)
                    tiles = wtiles(nn_)
                    mm_group(p[bank][:, 0:TT], rhs[i0:i0 + nn_], tiles,
                             (akeys if first else []), ("p", bank),
                             grp_start=first, grp_stop=(i0 + nn_ == NJ))
                    i0 += nn_
                    first = False
                if prevj[0] is not None:
                    residual_block(*prevj[0])
                prevj[0] = (c, bank, cur)
            residual_block(*prevj[0])

        pend0_seen.update([0, 1])
        emit_stage_cast(0)
        emit_stage_cast(1)

        for tau in range(NT):
            cur = tau % 2
            nxt = 1 - cur

            def src(pt, cur=cur):
                return xt[cur][:, pt * D:(pt + 1) * D]
            layernorm(src, PT, 0, 0, ("xt", cur), "ln", stati)
            if tau == 0:
                halo_ln()
            transpose_to_xlt("xlt")
            if tau + 1 < NT and tau > 1:
                ac_op("xd", mk_xload(nxt, tau + 1), writes=[("xt", nxt)])
            attn_up(tau)
            attn_down(tau, cur)
            layernorm(src, PT, 0, 0, ("xt", cur), "ln", stati,
                      from_partials=True)
            transpose_to_xlt("xlt")
            shared_pre(tau)
            router(tau)
            shared_post(tau)
            experts_up(tau)
            ffn_down(tau, cur)

            def mk_store(cur, tau):
                def f(s, cur=cur, tau=tau):
                    dst = out_h[tau * TT:(tau + 1) * TT, :].rearrange(
                        "(t p) d -> p t d", p=128)
                    return s.dma_start(dst, xt[cur][:].rearrange(
                        "p (t d) -> p t d", t=PT))
                return f
            ac_op("st", mk_store(cur, tau), reads=[("xt", cur)])
            if tau == 0 and NT > 1:
                ac_op("xd", mk_xload(1, 1),
                      writes=[("xt", 1)],
                      wars=[("stg", NCH - 2), ("stg", NCH - 1), "halo"])
            if tau == 1 and NT > 2:
                ac_op("xd", mk_xload(0, 2),
                      writes=[("xt", 0)],
                      wars=[("stg", 2 * NCH - 2), ("stg", 2 * NCH - 1)])

        assert wt[0] == cfg.n_pass * NT, (wt[0], cfg.n_pass * NT)

        S = sems

        def emit_waits(h, waits):
            for s, v in waits:
                h.wait_ge(S[s], v)

        @block.tensor
        def _(t):
            for waits, fn in em.ops["pe"]:
                emit_waits(t, waits)
                fn(t).then_inc(S["pe"], 1)

        @block.vector
        def _(v):
            for waits, fn in em.ops["ve"]:
                emit_waits(v, waits)
                fn(v).then_inc(S["ve"], 1)

        @block.scalar
        def _(a):
            for entry in em.ops["ae"]:
                if len(entry) == 3:
                    waits, fn, _ = entry
                    emit_waits(a, waits)
                    fn(a)
                else:
                    waits, fn = entry
                    emit_waits(a, waits)
                    fn(a).then_inc(S["ae"], 1)

        @block.gpsimd
        def _(g):
            g.memset(ident_b[:], 0.0).then_inc(S["po"], 1)
            g.wait_ge(S["po"], 1)
            g.affine_select(out=ident_b[:], in_=ident_b[:],
                            compare_op=ALU.not_equal, fill=1.0, base=0,
                            pattern=[[-1, 128]],
                            channel_multiplier=1).then_inc(S["po"], 1)
            g.memset(ident_f[:], 0.0).then_inc(S["po"], 1)
            g.wait_ge(S["po"], 3)
            g.affine_select(out=ident_f[:], in_=ident_f[:],
                            compare_op=ALU.not_equal, fill=1.0, base=0,
                            pattern=[[-1, 128]],
                            channel_multiplier=1).then_inc(S["po"], 1)
            g.memset(ones1[:], 1.0).then_inc(S["po"], 1)
            g.memset(icon[:], MAGIC).then_inc(S["po"], 1)
            for waits, fn in em.ops["po"]:
                emit_waits(g, waits)
                fn(g).then_inc(S["po"], 1)

        @block.sync
        def _(s):
            syhi = {}
            for waits, fn, sem, ser, lwars in sync_stream:
                allw = dict(waits)
                for k in lwars:
                    for s2, v2 in em.lastr.get(k, []):
                        allw[s2] = max(allw.get(s2, 0), v2)
                    if k in em.lastw:
                        s2, v2 = em.lastw[k]
                        allw[s2] = max(allw.get(s2, 0), v2)
                for s2, v2 in sorted(allw.items()):
                    if syhi.get(s2, 0) < v2:
                        s.wait_ge(S[s2], v2)
                        syhi[s2] = v2
                fn(s).then_inc(S[sem], 16)
                if ser is not None:
                    s.wait_ge(S[sem], ser)
            for sname, cnt_ in sorted(sy_cnt.items()):
                if sname != "st":
                    s.wait_ge(S[sname], 16 * cnt_)
            s.wait_ge(S["st"], 16 * st_dma[0])

    return nc


def make_host_arrays(cfg, w_in, w_out, conv_w, conv_b, Dvec, w_gate,
                     w_gate_up, w_down, w_sh_gate_up, w_sh_down):
    mats = {
        "w_inT": np.ascontiguousarray(np.asarray(w_in, np.float32).T),
        "w_outT": np.ascontiguousarray(np.asarray(w_out, np.float32).T),
        "w_sh_gate_up": np.asarray(w_sh_gate_up, np.float32),
        "w_sh_down": np.asarray(w_sh_down, np.float32),
    }
    for e in range(cfg.NE):
        mats["w_gate_up_%d" % e] = np.asarray(w_gate_up, np.float32)[e]
        mats["w_down_%d" % e] = np.asarray(w_down, np.float32)[e]
    ws = pack_wstream(cfg, mats)
    cwt = np.asarray(conv_w, np.float32)[:, 0, :]
    cw = np.ascontiguousarray(
        cwt.reshape(cfg.FXI, 128, 4).transpose(1, 0, 2).reshape(128, cfg.FXI * 4))
    cb = np.ascontiguousarray(
        np.asarray(conv_b, np.float32).reshape(cfg.FXI, 128).T)
    dd = np.ascontiguousarray(
        np.asarray(Dvec, np.float32).reshape(cfg.FXI, 128).T)
    wg = np.ascontiguousarray(
        np.asarray(w_gate, np.float32).reshape(cfg.KD, 128, cfg.NE)
        .transpose(1, 0, 2).reshape(128, cfg.KD * cfg.NE))
    return ws, cw, cb, dd, wg


def shard_x(cfg, x, b, t0):
    Dm = x.shape[-1]
    if t0 == 0:
        return np.ascontiguousarray(
            np.concatenate([np.zeros((3, Dm), np.float32),
                            x[b, 0:cfg.TC]], axis=0))
    return np.ascontiguousarray(x[b, t0 - 3: t0 + cfg.TC])


def kernel(L0_attn_x, L0_attn_w_in, L0_attn_w_out, L0_attn_conv_w,
           L0_attn_conv_b, L0_attn_D, L0_ffn_w_gate, L0_ffn_w_gate_up,
           L0_ffn_w_down, L0_ffn_w_sh_gate_up, L0_ffn_w_sh_down):
    cfg = REAL
    x = np.asarray(L0_attn_x, dtype=np.float32)
    B, T, Dm = x.shape
    n_cores = 8
    per_b = n_cores // B
    ws, cw, cb, dd, wg = make_host_arrays(
        cfg, L0_attn_w_in, L0_attn_w_out, L0_attn_conv_w, L0_attn_conv_b,
        L0_attn_D, L0_ffn_w_gate, L0_ffn_w_gate_up, L0_ffn_w_down,
        L0_ffn_w_sh_gate_up, L0_ffn_w_sh_down)
    in_maps = []
    for c in range(n_cores):
        b, q = divmod(c, per_b)
        in_maps.append({"x": shard_x(cfg, x, b, q * cfg.TC), "ws": ws,
                        "cw": cw, "cb": cb, "dd": dd, "wg": wg})
    nc = build_program(cfg)
    res = run_bass_kernel_spmd(nc, in_maps, core_ids=list(range(n_cores)))
    out = np.empty((B, T, Dm), np.float32)
    for c in range(n_cores):
        b, q = divmod(c, per_b)
        out[b, q * cfg.TC:(q + 1) * cfg.TC] = res.results[c]["out"]
    return out


# version 16
# speedup vs baseline: 1.1531x; 1.1531x over previous
import numpy as np
import concourse.bass as bass
import concourse.mybir as mybir
from concourse.bass_utils import run_bass_kernel_spmd

F32 = mybir.dt.float32
BF16 = mybir.dt.bfloat16
I32 = mybir.dt.int32
AF = mybir.ActivationFunctionType
ALU = mybir.AluOpType
AX = mybir.AxisListType

MAGIC = 0x5F3759DF
EPS = 1e-5


class Cfg:
    def __init__(self, D, DI, NE, IM, ISH, TC, TT, CT):
        self.D, self.DI, self.NE, self.IM, self.ISH = D, DI, NE, IM, ISH
        self.TC, self.TT, self.CT = TC, TT, CT
        self.KD = D // 128
        self.FXI = DI // 128
        self.FH = IM // 128
        self.FSH = ISH // 128
        self.NT = TC // TT
        self.PT = TT // 128
        self.CO = D // 128
        self.UPT = D // TT
        assert D % TT == 0
        self.n_D = self.FXI * 2 * self.KD
        self.n_E = self.CO * self.FXI
        self.n_I = self.FSH * 2 * self.KD
        self.n_H = NE * self.FH * 2 * self.KD
        self.n_J = self.CO * (NE * self.FH + self.FSH)
        self.n_pass = self.n_D + self.n_E + self.n_I + self.n_H + self.n_J
        self.n_chunks = -(-self.n_pass // CT)


REAL = Cfg(D=2048, DI=4096, NE=8, IM=1024, ISH=4096, TC=2048, TT=512, CT=24)
MINI = Cfg(D=256, DI=512, NE=4, IM=128, ISH=256, TC=256, TT=128, CT=6)
NSLOT = 3


def stream_tile_sources(cfg):
    KD, FXI, FH, FSH, NE, CO = cfg.KD, cfg.FXI, cfg.FH, cfg.FSH, cfg.NE, cfg.CO
    for f in range(FXI):
        for k in range(KD):
            yield ("w_inT", k * 128, f * 128)
        for k in range(KD):
            yield ("w_inT", k * 128, cfg.DI + f * 128)
    for c in range(CO):
        for k in range(FXI):
            yield ("w_outT", k * 128, c * 128)
    for f in range(FSH):
        for k in range(KD):
            yield ("w_sh_gate_up", k * 128, f * 128)
        for k in range(KD):
            yield ("w_sh_gate_up", k * 128, cfg.ISH + f * 128)
    for e in range(NE):
        for f in range(FH):
            for k in range(KD):
                yield ("w_gate_up_%d" % e, k * 128, f * 128)
            for k in range(KD):
                yield ("w_gate_up_%d" % e, k * 128, cfg.IM + f * 128)
    for c in range(CO):
        for e in range(NE):
            for k in range(FH):
                yield ("w_down_%d" % e, k * 128, c * 128)
        for k in range(FSH):
            yield ("w_sh_down", k * 128, c * 128)


def pack_wstream(cfg, mats):
    CT = cfg.CT
    out = np.zeros((cfg.n_chunks, 128, CT * 128), dtype=np.float32)
    for i, (name, r0, c0) in enumerate(stream_tile_sources(cfg)):
        ch, t = divmod(i, CT)
        out[ch, :, t * 128:(t + 1) * 128] = mats[name][r0:r0 + 128, c0:c0 + 128]
    return out


class Emitter:

    def __init__(self):
        self.ops = {"pe": [], "ve": [], "ae": [], "po": []}
        self.cnt = {"pe": 0, "ve": 0, "ae": 0, "po": 6}
        self.lastw = {}
        self.lastr = {}
        self.hi = {"pe": {}, "ve": {}, "ae": {}, "po": {}, "sy": {}}

    def preset_write(self, key, sem_name, value):
        self.lastw[key] = (sem_name, value)
        self.lastr[key] = []

    def note_read(self, key, sem_name, value):
        self.lastr.setdefault(key, []).append((sem_name, value))

    def waits_for(self, eng, reads, writes, wars=()):
        need = {}
        for k in reads:
            if k in self.lastw:
                s, v = self.lastw[k]
                need[s] = max(need.get(s, 0), v)
        for k in wars:
            for s, v in self.lastr.get(k, []):
                need[s] = max(need.get(s, 0), v)
            if k in self.lastw:
                s, v = self.lastw[k]
                need[s] = max(need.get(s, 0), v)
        for k in writes:
            if k in self.lastw:
                s, v = self.lastw[k]
                need[s] = max(need.get(s, 0), v)
            for s, v in self.lastr.get(k, []):
                need[s] = max(need.get(s, 0), v)
        out = []
        for s, v in sorted(need.items()):
            if self.hi[eng].get(s, 0) < v:
                out.append((s, v))
                self.hi[eng][s] = v
        return out

    def op(self, eng, fn, reads=(), writes=(), n_incs=1, wars_extra=()):
        waits = self.waits_for(eng, reads, writes, wars_extra)
        self.cnt[eng] += n_incs
        mark = self.cnt[eng]
        self.ops[eng].append((waits, fn))
        for k in writes:
            self.lastw[k] = (eng, mark)
            self.lastr[k] = []
        for k in reads:
            self.note_read(k, eng, mark)
        return mark


def build_program(cfg):
    D, DI, NE, IM, ISH = cfg.D, cfg.DI, cfg.NE, cfg.IM, cfg.ISH
    TC, TT, CT = cfg.TC, cfg.TT, cfg.CT
    KD, FXI, FH, FSH, NT, PT, CO, UPT = (cfg.KD, cfg.FXI, cfg.FH, cfg.FSH,
                                         cfg.NT, cfg.PT, cfg.CO, cfg.UPT)
    NCH = cfg.n_chunks
    total_chunks = NCH * NT
    NU = NE * FH + FSH
    BIGC = max(NU * TT, FXI * TT, (2 * PT + 1) * D + 2 * D)

    nc = bass.Bass("TRN2", target_bir_lowering=False)
    x_h = nc.dram_tensor("x", [TC + 3, D], F32, kind="ExternalInput").ap()
    ws_h = nc.dram_tensor("ws", [NCH, 128, CT * 128], F32, kind="ExternalInput").ap()
    cw_h = nc.dram_tensor("cw", [128, FXI * 4], F32, kind="ExternalInput").ap()
    cb_h = nc.dram_tensor("cb", [128, FXI], F32, kind="ExternalInput").ap()
    dd_h = nc.dram_tensor("dd", [128, FXI], F32, kind="ExternalInput").ap()
    wg_h = nc.dram_tensor("wg", [128, KD * NE], F32, kind="ExternalInput").ap()
    out_h = nc.dram_tensor("out", [TC, D], F32, kind="ExternalOutput").ap()
    wsb_h = nc.dram_tensor("wsb", [NCH, 128, CT * 128], BF16).ap()

    from contextlib import ExitStack
    with ExitStack() as ctx:
        en = ctx.enter_context
        xt = [en(nc.sbuf_tensor("xt%d" % i, [128, PT * D], F32)) for i in range(2)]
        xlt = en(nc.sbuf_tensor("xlt", [128, KD * TT], BF16))
        xlh = en(nc.sbuf_tensor("xlh", [128, KD * 3], BF16))
        big = en(nc.sbuf_tensor("big", [128, BIGC], BF16))
        xib = en(nc.sbuf_tensor("xib", [128, 2 * (TT + 4)], BF16))
        tail = en(nc.sbuf_tensor("tail", [128, FXI * 3], BF16))
        wbuf = en(nc.sbuf_tensor("wbuf", [128, NSLOT * CT * 128], BF16))
        diag = en(nc.sbuf_tensor("diag", [128, 4 * 128], BF16))
        sgb = en(nc.sbuf_tensor("sgb", [128, 2 * TT], BF16))
        ev = en(nc.sbuf_tensor("ev", [128, 2 * TT], BF16))
        ident_b = en(nc.sbuf_tensor("ident_b", [128, 128], BF16))
        ident_f = en(nc.sbuf_tensor("ident_f", [128, 128], F32))
        ones1 = en(nc.sbuf_tensor("ones1", [1, 128], BF16))
        cw = en(nc.sbuf_tensor("cw_s", [128, FXI * 4], F32))
        cb = en(nc.sbuf_tensor("cb_s", [128, FXI], F32))
        dd = en(nc.sbuf_tensor("dd_s", [128, FXI], F32))
        wg = en(nc.sbuf_tensor("wg_s", [128, KD * NE], BF16))
        stat = en(nc.sbuf_tensor("stat", [128, 32], F32))
        stat2 = en(nc.sbuf_tensor("stat2", [128, 2 * 4 * 16], F32))
        stati = en(nc.sbuf_tensor("stati", [128, 4], I32))
        statih = en(nc.sbuf_tensor("statih", [3, 1], I32))
        icon = en(nc.sbuf_tensor("icon", [128, 4], I32))
        rtm = en(nc.sbuf_tensor("rtm", [128, PT * NE], F32))
        ctm = en(nc.sbuf_tensor("ctm", [128, PT * NE], F32))
        rt1 = en(nc.sbuf_tensor("rt1", [128, PT * NE], F32))
        rt2 = en(nc.sbuf_tensor("rt2", [128, PT * NE], F32))
        rsg = en(nc.sbuf_tensor("rsg", [128, PT * NE], F32))
        rst = en(nc.sbuf_tensor("rst", [128, 16], F32))
        cem = en(nc.sbuf_tensor("cem", [NE, TT], BF16))
        crow = en(nc.sbuf_tensor("crow", [1, TT], BF16))
        bce = en(nc.sbuf_tensor("bce", [128, TT], BF16))
        p = [en(nc.psum_tensor("p%d" % i, [128, 512], F32)) for i in range(6)]
        pb = [en(nc.psum_tensor("pb%d" % i, [128, 1024], BF16)) for i in range(2)]
        sems = {}
        for s in ["pe", "ve", "ae", "po", "sd0", "sd1", "wh0", "wh1",
                  "wh2", "wh3", "wh4", "wh5", "wb0", "wb1", "wb2", "wb3",
                  "wb4", "wb5", "wgd", "xd", "cs", "hl", "cr0", "cr1", "st"]:
            sems[s] = en(nc.semaphore(s))
        block = en(nc.Block())

        if PT * D >= 2 * CT * 128:
            stg_dedicated = None
        else:
            stg_dedicated = en(nc.sbuf_tensor("stg", [128, 2 * CT * 128], F32))

        def stg_ap(slot, b=0):
            if stg_dedicated is not None:
                return stg_dedicated[:, slot * CT * 128:(slot + 1) * CT * 128]
            return xt[1 - (b // NCH)][:, slot * CT * 128:(slot + 1) * CT * 128]

        em = Emitter()
        sync_stream = []
        sy_dma = [0]
        st_dma = [0]

        sy_cnt = {}

        def sy_op(sem, fn, reads=(), writes=(), wars=(), serialize=False,
                  lazy_wars=(), writes_clean=()):
            waits = em.waits_for("sy", reads, writes, wars)
            sy_cnt[sem] = sy_cnt.get(sem, 0) + 1
            mark = 16 * sy_cnt[sem]
            if sem == "st":
                st_dma[0] += 1
            waits = [(s, v) for s, v in waits if not (s == sem and v >= mark)]
            sync_stream.append((waits, fn, sem, mark if serialize else None,
                                tuple(lazy_wars)))
            for k in writes_clean:
                em.lastw[k] = (sem, mark)
            for k in writes:
                em.lastw[k] = (sem, mark)
                em.lastr[k] = []
            for k in reads:
                em.note_read(k, sem, mark)
            return mark


        def pe_op(fn, reads=(), writes=()):
            return em.op("pe", fn, reads=reads, writes=writes, n_incs=1)

        def ac_op(sem, fn, reads=(), writes=(), wars=(), serialize=False):
            waits = em.waits_for("ae", reads, writes, wars)
            sy_cnt[sem] = sy_cnt.get(sem, 0) + 1
            mark = 16 * sy_cnt[sem]
            if sem == "st":
                st_dma[0] += 1
            waits = [(s, v) for s, v in waits if not (s == sem and v >= mark)]

            def wrapped(a, fn=fn, sem=sem, waits=waits, mark=mark,
                        serialize=serialize):
                ins = fn(a)
                ins.then_inc(sems[sem], 16)
                if serialize:
                    a.wait_ge(sems[sem], mark)
                return None
            em.ops["ae"].append((waits, wrapped, True))
            for k in writes:
                em.lastw[k] = (sem, mark)
                em.lastr[k] = []
            for k in reads:
                em.note_read(k, sem, mark)
            return mark

        def ve_op(fn, reads=(), writes=(), n=1):
            return em.op("ve", fn, reads=reads, writes=writes, n_incs=n)

        def ae_op(fn, reads=(), writes=()):
            return em.op("ae", fn, reads=reads, writes=writes, n_incs=1)

        em.preset_write("ident_b", "po", 2)
        em.preset_write("ident_f", "po", 4)
        em.preset_write("ones1", "po", 5)
        em.preset_write("icon", "po", 6)

        _whcnt = {}
        for a in range(total_chunks):
            if a >= NCH and not (a < 2 * NCH and (a % NCH) >= (NCH // 3 if NT > 1 else NCH)):
                sname = "wh%d" % (a % NSLOT)
                _whcnt[sname] = _whcnt.get(sname, 0) + 1
                em.preset_write(("wch", a), sname, 16 * _whcnt[sname])


        wg_stg = ev[:, 0:2 * KD * NE].bitcast(F32)
        ac_op("cs", lambda s: s.dma_start(wg_stg, wg_h),
              writes=[("ev", 0), ("ev", 1), "wg_stg"])
        ve_op(lambda v: v.tensor_copy(wg[:], wg_stg),
              reads=["wg_stg"], writes=["wg"])
        ac_op("cr0", lambda s: s.dma_start(cw[:], cw_h), writes=["cw"])
        ac_op("cr1", lambda s: s.dma_start(cb[:], cb_h), writes=["cb"])
        ac_op("wgd", lambda s: s.dma_start(dd[:], dd_h), writes=["dd"])

        def mk_xload(buf_i, tau):
            def f(s, buf_i=buf_i, tau=tau):
                dst = xt[buf_i][:].rearrange("p (t d) -> p t d", t=PT)
                src = x_h[3 + tau * TT: 3 + (tau + 1) * TT, :].rearrange(
                    "(t p) d -> p t d", p=128)
                return s.dma_start(dst, src)
            return f
        ac_op("xd", mk_xload(0, 0), writes=[("xt", 0)], serialize=True)
        HB0 = (PT + 1 + PT) * D
        halo_v = big[0:3, HB0: HB0 + 2 * D].bitcast(F32)
        hrkeys = [("big", (HB0 + u * TT) // TT) for u in range(2 * UPT)]
        ac_op("hl", lambda s: s.dma_start(halo_v, x_h[0:3, :]),
              writes=["halo"] + hrkeys)

        wt = [0]
        claimed = set()
        wb_cnt = [0]
        pend_loads = []
        pend_stage = []
        pend0_seen = set()

        def flush_stage():
            pend_stage.sort()
            while pend_stage and wt[0] >= (pend_stage[0] - 2) * CT:
                emit_stage_cast(pend_stage.pop(0))

        def flush_wloads():
            flush_stage()
            while pend_loads and wt[0] >= (pend_loads[0] - NSLOT + 1) * CT:
                b = pend_loads.pop(0)
                slot = b % NSLOT

                def ld(s, b=b, slot=slot):
                    return s.dma_start(
                        wbuf[:, slot * CT * 128:(slot + 1) * CT * 128],
                        wsb_h[b % NCH])
                sy_op("wh%d" % (b % NSLOT), ld,
                      reads=[("wsb", b % NCH)], wars=[("wch", b - NSLOT)],
                      writes_clean=[("wch", b)])

        WBCUT = NCH // 3 if NT > 1 else NCH

        def emit_stage_cast(b):
            sslot = b % 2
            wslot = b % NSLOT

            def ld(s, b=b, sslot=sslot):
                return s.dma_start(stg_ap(sslot, b), ws_h[b % NCH])
            extra_wars = [("stg", b - 2)]
            if b in (NCH + WBCUT, NCH + WBCUT + 1):
                extra_wars = [("xt", 0)]
            sy_op("sd%d" % sslot, ld, wars=extra_wars,
                  writes_clean=[("stg", b)])
            eng = "ve" if (b % 3) < 2 else "ae"

            def cast(h, b=b, sslot=sslot, wslot=wslot, eng=eng):
                dst = wbuf[:, wslot * CT * 128:(wslot + 1) * CT * 128]
                if eng == "ae":
                    return h.copy(dst, stg_ap(sslot, b))
                return h.tensor_copy(dst, stg_ap(sslot, b))
            em.op(eng, cast, reads=[("stg", b)],
                  writes=[("wch", b)], wars_extra=[("wch", b - NSLOT)])

        def is_staged(a):
            return a < NCH or (a < 2 * NCH and (a % NCH) >= WBCUT)

        def needs_wb(a):
            return (a < NCH and (a % NCH) < WBCUT) or                    (NCH <= a < 2 * NCH and (a % NCH) >= WBCUT)

        def on_first_claim(a):
            if needs_wb(a):
                slot = a % NSLOT
                wb_cnt[0] += 1

                def wbk(s, a=a, slot=slot):
                    return s.dma_start(
                        wsb_h[a % NCH],
                        wbuf[:, slot * CT * 128:(slot + 1) * CT * 128])
                sy_op("wb%d" % (a % NSLOT), wbk, reads=[("wch", a)],
                      writes=[("wsb", a % NCH)])
            for b in (a + 2, a + NSLOT):
                if 2 <= b < total_chunks and is_staged(b) and b not in pend0_seen:
                    pend0_seen.add(b)
                    pend_stage.append(b)
            b = a + NSLOT
            if b < total_chunks and not is_staged(b):
                pend_loads.append(b)

        def wtiles(n):
            t0 = wt[0]
            wt[0] += n
            out = []
            for t in range(t0, t0 + n):
                a = t // CT
                if a not in claimed:
                    claimed.add(a)
                    on_first_claim(a)
                out.append((wbuf[:, (t % (NSLOT * CT)) * 128:
                                 ((t % (NSLOT * CT)) + 1) * 128], a))
            return out

        def mm_group(psum_ap, rhs_list, tiles, extra_reads, wkey,
                     no_flush=False, grp_start=True, grp_stop=True):
            n = len(tiles)
            parts = []
            cur = [0]
            for i in range(1, n):
                if tiles[i][1] != tiles[i - 1][1]:
                    parts.append(cur)
                    cur = []
                cur.append(i)
            parts.append(cur)
            first_extra = list(extra_reads)
            for pi, idxs in enumerate(parts):
                chunks = sorted({tiles[i][1] for i in idxs})
                ckeys = [("wch", c) for c in chunks]
                is_first = pi == 0
                is_last = pi == len(parts) - 1

                def f(t, idxs=idxs, is_first=is_first, is_last=is_last,
                      psum_ap=psum_ap, rhs_list=rhs_list, tiles=tiles, n=n,
                      grp_start=grp_start, grp_stop=grp_stop):
                    ins = None
                    for j, i in enumerate(idxs):
                        ins = t.matmul(psum_ap, tiles[i][0], rhs_list[i],
                                       start=(grp_start and is_first and j == 0),
                                       stop=(grp_stop and is_last
                                             and j == len(idxs) - 1))
                    return ins
                pe_op(f, reads=ckeys + (first_extra if is_first else []),
                      writes=[wkey])
                first_extra = []
            if not no_flush:
                flush_wloads()

        def layernorm(src_fn, n_pt, sbase, dump_u0, key_src, key_out,
                      int_sc, part=128, from_partials=False):
            sb = sbase
            dump_col = dump_u0 * TT
            upp = D // TT
            if from_partials:
                for pt in range(n_pt):
                    pk = ([("s1p", pt, c) for c in range(CO)]
                          + [("s2p", pt, c) for c in range(CO)])
                    ve_op(lambda v, pt=pt: v.reduce_sum(
                        stat[:, sb + pt: sb + pt + 1],
                        stat2[:, pt * CO:(pt + 1) * CO], axis=AX.X),
                        reads=pk, writes=[("st1", sb, pt)])
                    ve_op(lambda v, pt=pt: v.reduce_sum(
                        stat[:, sb + n_pt + pt: sb + n_pt + pt + 1],
                        stat2[:, PT * CO + pt * CO: PT * CO + (pt + 1) * CO],
                        axis=AX.X),
                        reads=pk, writes=[("st2", sb, pt)])
            else:
                for pt in range(n_pt):
                    ukeys = [("big", dump_u0 + pt * upp + u) for u in range(upp)]
                    ae_op(lambda a, pt=pt: a.activation(
                        big[0:part, dump_col + pt * D: dump_col + (pt + 1) * D],
                        src_fn(pt), AF.Square,
                        accum_out=stat[0:part, sb + n_pt + pt: sb + n_pt + pt + 1]),
                        reads=[key_src], writes=ukeys + [("st2", sb, pt)])
                    sdc = (PT + 1) * D + pt * D
                    sukeys = [("big", (sdc // TT) + u) for u in range(upp)]
                    ve_op(lambda v, pt=pt, sdc=sdc: v.tensor_scalar(
                        big[0:part, sdc: sdc + D],
                        src_fn(pt), 1.0, None, ALU.mult, ALU.add,
                        accum_out=stat[0:part, sb + pt: sb + pt + 1]),
                        reads=[key_src],
                        writes=[("st1", sb, pt)] + sukeys)
            def fin_apply(p0, p1):
                npg = p1 - p0
                s1 = stat[0:part, sb + p0: sb + p1]
                s2 = stat[0:part, sb + n_pt + p0: sb + n_pt + p1]
                mean = stat[0:part, sb + 2 * n_pt + p0: sb + 2 * n_pt + p1]
                var = stat[0:part, sb + 3 * n_pt + p0: sb + 3 * n_pt + p1]
                skeys = ([("st1", sb, pt) for pt in range(p0, p1)]
                         + [("st2", sb, pt) for pt in range(p0, p1)])
                fk = [("fin", sb, p0)]
                isc = int_sc[0:part, p0:p1]
                iconp = icon[0:part, p0:p1]
                steps = [
                    lambda v: v.tensor_scalar(mean, s1, 1.0 / D, None, ALU.mult),
                    lambda v: v.tensor_scalar(var, s2, 1.0 / D, EPS, ALU.mult,
                                              ALU.add),
                    lambda v: v.tensor_tensor(s2, mean, mean, ALU.mult),
                    lambda v: v.tensor_tensor(var, var, s2, ALU.subtract),
                    lambda v: v.tensor_scalar(isc, var.bitcast(I32), 1, None,
                                              ALU.arith_shift_right),
                    lambda v: v.tensor_tensor(isc, iconp, isc, ALU.subtract),
                    lambda v: v.tensor_copy(s1, isc.bitcast(F32)),
                ]
                for _ in range(2):
                    steps += [
                        lambda v: v.tensor_tensor(s2, s1, s1, ALU.mult),
                        lambda v: v.tensor_tensor(s2, s2, var, ALU.mult),
                        lambda v: v.tensor_scalar(s2, s2, -0.5, 1.5, ALU.mult,
                                                  ALU.add),
                        lambda v: v.tensor_tensor(s1, s1, s2, ALU.mult),
                    ]
                for st_ in steps:
                    ve_op(st_, reads=skeys + fk + ["icon"], writes=fk)
                inv = s1
                for pt in range(p0, p1):
                    ukeys = [("big", dump_u0 + pt * upp + u) for u in range(upp)]
                    ve_op(lambda v, pt=pt: v.tensor_scalar(
                        big[0:part, dump_col + pt * D: dump_col + (pt + 1) * D],
                        src_fn(pt),
                        mean[0:part, pt - p0: pt - p0 + 1],
                        inv[0:part, pt - p0: pt - p0 + 1],
                        ALU.subtract, ALU.mult),
                        reads=fk + [key_src], writes=ukeys + [(key_out, pt)])
            fin_apply(0, n_pt)

        def transpose_to_xlt(key_out):
            nb = [0]
            upp = D // TT
            for pt in range(PT):
                for kb0 in range(0, KD, 8):
                    kn = min(8, KD - kb0)
                    bank = nb[0] % 2
                    nb[0] += 1
                    ukeys = [("big", pt * upp + u) for u in range(upp)]

                    def f(t, pt=pt, kb0=kb0, kn=kn, bank=bank):
                        ins = None
                        for i in range(kn):
                            ins = t.transpose(
                                pb[bank][:, i * 128:(i + 1) * 128],
                                big[:, pt * D + (kb0 + i) * 128:
                                    pt * D + (kb0 + i + 1) * 128],
                                ident_b[:])
                        return ins
                    pe_op(f, reads=ukeys + [("ln", pt), "ident_b"],
                          writes=[("pb", bank)])

                    def g(v, pt=pt, kb0=kb0, kn=kn, bank=bank):
                        dst = xlt[:].rearrange("p (k t) -> p k t", k=KD)[
                            :, kb0:kb0 + kn, pt * 128:(pt + 1) * 128]
                        src = pb[bank][:, 0:kn * 128].rearrange(
                            "p (k m) -> p k m", k=kn)
                        return v.tensor_copy(dst, src)
                    ve_op(g, reads=[("pb", bank)], writes=["xlt"])

        def halo_ln():
            src = lambda pt: halo_v
            layernorm(src, 1, 24, PT * UPT, "halo", "hln", statih, part=3)
            hu = [("big", PT * UPT + u) for u in range(UPT)]

            def f(t):
                ins = None
                for k in range(KD):
                    ins = t.transpose(
                        pb[0][0:128, k * 4: k * 4 + 3],
                        big[0:3, PT * D + k * 128: PT * D + (k + 1) * 128],
                        ident_b[0:3, 0:3])
                return ins
            pe_op(f, reads=hu + [("hln", 0), "ident_b"], writes=[("pb", 0)])

            def g(v):
                dst = xlh[:].rearrange("p (k c) -> p k c", k=KD)
                srcv = pb[0][:, 0:KD * 4].rearrange(
                    "p (k c) -> p k c", k=KD)[:, :, 0:3]
                return v.tensor_copy(dst, srcv)
            ve_op(g, reads=[("pb", 0)], writes=["xlh"])

        def attn_up(tau):
            for f in range(FXI):
                xi_bank = 0 if tau == 0 else (f % 2)
                g_bank = 2 + (f % 2)
                xc_bank = 4 + (f % 2)
                xb = f % 2
                sga = f % 2
                rhs = [xlt[:].rearrange("p (k t) -> p k t", k=KD)[:, k, :]
                       for k in range(KD)]
                xi_tiles = wtiles(KD)
                mm_group(p[xi_bank][:, 0:TT], rhs, xi_tiles,
                         ["xlt"], ("p", xi_bank), no_flush=(tau == 0))
                if tau == 0:
                    rhs_h = [xlh[:, k * 3:(k + 1) * 3] for k in range(KD)]
                    mm_group(p[1][:, 0:3], rhs_h, xi_tiles, ["xlh"], ("p", 1))
                    ve_op(lambda v, f=f: v.tensor_copy(
                        tail[:, f * 3:(f + 1) * 3], p[1][:, 0:3]),
                        reads=[("p", 1)], writes=[("tail", f)])
                g_tiles = wtiles(KD)
                mm_group(p[g_bank][:, 0:TT], rhs, g_tiles, ["xlt"], ("p", g_bank))
                ve_op(lambda v, f=f, xb=xb: v.tensor_copy(
                    xib[:, xb * (TT + 4): xb * (TT + 4) + 3],
                    tail[:, f * 3:(f + 1) * 3]),
                    reads=[("tail", f)], writes=[("xib", xb)])
                ve_op(lambda v, xb=xb, xi_bank=xi_bank: v.tensor_copy(
                    xib[:, xb * (TT + 4) + 3: xb * (TT + 4) + 3 + TT],
                    p[xi_bank][:, 0:TT]),
                    reads=[("p", xi_bank)], writes=[("xib", xb)])
                ve_op(lambda v, f=f, xb=xb: v.tensor_copy(
                    tail[:, f * 3:(f + 1) * 3],
                    xib[:, xb * (TT + 4) + TT: xb * (TT + 4) + TT + 3]),
                    reads=[("xib", xb)], writes=[("tail", f)])

                def mkdiag(v, f=f):
                    ins = None
                    for j in range(4):
                        if ins is not None:
                            ins.then_inc(sems["ve"], 1)
                        ins = v.tensor_scalar(
                            diag[:, j * 128:(j + 1) * 128],
                            ident_b[:], cw[:, f * 4 + j: f * 4 + j + 1], None,
                            ALU.mult)
                    return ins
                ve_op(mkdiag, reads=["ident_b", "cw"], writes=["diag"], n=4)

                def conv(t, xb=xb, xc_bank=xc_bank):
                    ins = None
                    for j in range(4):
                        ins = t.matmul(
                            p[xc_bank][:, 0:TT],
                            diag[:, j * 128:(j + 1) * 128],
                            xib[:, xb * (TT + 4) + j: xb * (TT + 4) + j + TT],
                            start=(j == 0), stop=(j == 3))
                    return ins
                pe_op(conv, reads=["diag", ("xib", xb)],
                      writes=[("p", xc_bank)])
                ae_op(lambda a, g_bank=g_bank, sga=sga: a.activation(
                    sgb[:, sga * TT:(sga + 1) * TT], p[g_bank][:, 0:TT], AF.Silu),
                    reads=[("p", g_bank)], writes=[("sgb", sga)])
                ae_op(lambda a, xc_bank=xc_bank, sga=sga, f=f: a.activation(
                    ev[:, sga * TT:(sga + 1) * TT], p[xc_bank][:, 0:TT], AF.Silu,
                    bias=cb[:, f: f + 1]),
                    reads=[("p", xc_bank), "cb"], writes=[("ev", sga)])
                ve_op(lambda v, f=f, sga=sga: v.scalar_tensor_tensor(
                    big[:, f * TT:(f + 1) * TT],
                    ev[:, sga * TT:(sga + 1) * TT],
                    dd[:, f: f + 1],
                    sgb[:, sga * TT:(sga + 1) * TT],
                    ALU.mult, ALU.mult),
                    reads=[("ev", sga), ("sgb", sga), "dd"],
                    writes=[("big", f)])

        def residual_block(c, bank, cur, ln_stats=False):
            eb = bank
            ae_op(lambda a, bank=bank, eb=eb: a.copy(
                ev[:, eb * TT:(eb + 1) * TT], p[bank][:, 0:TT]),
                reads=[("p", bank)], writes=[("ev", eb)])

            def tr(t, eb=eb, bank=bank):
                ins = None
                for pt in range(PT):
                    ins = t.transpose(
                        pb[bank][:, pt * 128:(pt + 1) * 128],
                        ev[:, eb * TT + pt * 128: eb * TT + (pt + 1) * 128],
                        ident_b[:])
                return ins
            pe_op(tr, reads=[("ev", eb), "ident_b"], writes=[("pb", bank)])

            def add(v, c=c, bank=bank, cur=cur):
                dst = xt[cur][:].rearrange("p (t d) -> p t d", t=PT)[
                    :, :, c * 128:(c + 1) * 128]
                src = pb[bank][:, 0:PT * 128].rearrange("p (t m) -> p t m", t=PT)
                return v.tensor_tensor(dst, dst, src, ALU.add)
            ve_op(add, reads=[("pb", bank), ("xt", cur)], writes=[("xt", cur)])
            if ln_stats:
                for pt in range(PT):
                    xs_ = xt[cur][:, pt * D + c * 128: pt * D + (c + 1) * 128]
                    ae_op(lambda a, xs_=xs_, pt=pt, c=c: a.activation(
                        p[2][:, pt * 128:(pt + 1) * 128],
                        xs_, AF.Square,
                        accum_out=stat2[:, PT * CO + pt * CO + c:
                                        PT * CO + pt * CO + c + 1]),
                        reads=[("xt", cur)],
                        writes=[("s2p", pt, c), ("p", 2)]),
                    ve_op(lambda v, xs_=xs_, pt=pt, c=c: v.tensor_scalar(
                        p[3][:, pt * 128:(pt + 1) * 128],
                        xs_, 1.0, None, ALU.mult, ALU.add,
                        accum_out=stat2[:, pt * CO + c: pt * CO + c + 1]),
                        reads=[("xt", cur)],
                        writes=[("s1p", pt, c), ("p", 3)])

        def attn_down(tau, cur):
            hkeys = [("big", f) for f in range(FXI)]
            prev = None
            for c in range(CO):
                bank = c % 2
                tiles = wtiles(FXI)
                rhs = [big[:, k * TT:(k + 1) * TT] for k in range(FXI)]
                mm_group(p[bank][:, 0:TT], rhs, tiles, hkeys, ("p", bank))
                if prev is not None:
                    residual_block(*prev, ln_stats=True)
                prev = (c, bank, cur)
            residual_block(*prev, ln_stats=True)

        def router(tau):
            def f(t):
                ins = None
                for k in range(KD):
                    ins = t.matmul(p[0][0:NE, 0:TT],
                                   wg[:, k * NE:(k + 1) * NE],
                                   xlt[:].rearrange("p (k t) -> p k t", k=KD)[:, k, :],
                                   start=(k == 0), stop=(k == KD - 1))
                return ins
            pe_op(f, reads=["wg", "xlt"], writes=[("p", 0)])
            lre = ev[:].bitcast(F32)
            ve_op(lambda v: v.tensor_copy(lre[0:NE, 0:TT], p[0][0:NE, 0:TT]),
                  reads=[("p", 0)], writes=[("ev", 0), ("ev", 1)])

            def trl(t):
                ins = None
                for pt in range(PT):
                    ins = t.transpose(p[2][0:128, pt * NE:(pt + 1) * NE],
                                      lre[0:NE, pt * 128:(pt + 1) * 128],
                                      ident_f[0:NE, 0:NE])
                return ins
            pe_op(trl, reads=[("ev", 0), ("ev", 1), "ident_f"],
                  writes=[("p", 2)])
            ve_op(lambda v: v.tensor_copy(rtm[:, 0:PT * NE], p[2][:, 0:PT * NE]),
                  reads=[("p", 2)], writes=["rtm"])
            l3 = rtm[:].rearrange("p (g e) -> p g e", g=PT)
            m1 = rst[:, 0:PT]
            m2 = rst[:, PT:2 * PT]
            sb_ = rst[:, 2 * PT:3 * PT]
            m1b = m1.unsqueeze(2).broadcast_to([128, PT, NE])
            m2b = m2.unsqueeze(2).broadcast_to([128, PT, NE])
            ssb = sb_.unsqueeze(2).broadcast_to([128, PT, NE])
            rt1v = rt1[:].rearrange("p (g e) -> p g e", g=PT)
            rt2v = rt2[:].rearrange("p (g e) -> p g e", g=PT)
            rsgv = rsg[:].rearrange("p (g e) -> p g e", g=PT)
            ctmv = ctm[:].rearrange("p (g e) -> p g e", g=PT)
            ve_op(lambda v: v.reduce_max(m1, l3, axis=AX.X),
                  reads=["rtm"], writes=["rst"])
            ve_op(lambda v: v.tensor_tensor(rt1v, l3, m1b, ALU.is_ge),
                  reads=["rtm", "rst"], writes=["rt1"])
            ve_op(lambda v: v.scalar_tensor_tensor(
                rt1v, rt1v, -1e30, l3, ALU.mult, ALU.add),
                reads=["rt1", "rtm"], writes=["rt1"])
            ve_op(lambda v: v.reduce_max(m2, rt1v, axis=AX.X),
                  reads=["rt1"], writes=["rst"])
            ve_op(lambda v: v.tensor_tensor(rt2v, l3, m2b, ALU.is_ge),
                  reads=["rtm", "rst"], writes=["rt2"])
            ve_op(lambda v: v.tensor_tensor(sb_, m1, m2, ALU.add),
                  reads=["rst"], writes=["rst"])
            ve_op(lambda v: v.tensor_scalar(rsgv, l3, 2.0, None, ALU.mult),
                  reads=["rtm"], writes=["rsg"])
            ve_op(lambda v: v.tensor_tensor(rsgv, rsgv, ssb, ALU.subtract),
                  reads=["rsg", "rst"], writes=["rsg"])
            ae_op(lambda a: a.activation(rsg[:], rsg[:], AF.Sigmoid),
                  reads=["rsg"], writes=["rsg"])
            ve_op(lambda v: v.tensor_tensor(ctmv, rsgv, rt2v, ALU.mult),
                  reads=["rsg", "rt2"],
                  writes=[("ctm", pt) for pt in range(PT)])

            def trc(t):
                ins = None
                for pt in range(PT):
                    ins = t.transpose(p[3][0:NE, pt * 128:(pt + 1) * 128],
                                      ctm[:, pt * NE:(pt + 1) * NE],
                                      ident_f[:])
                return ins
            pe_op(trc, reads=[("ctm", pt) for pt in range(PT)] + ["ident_f"],
                  writes=[("p", 3)])
            ve_op(lambda v: v.tensor_copy(cem[:, 0:TT], p[3][0:NE, 0:TT]),
                  reads=[("p", 3)], writes=["cem"])
            pass

        def up_pair(f_rhs, g_bank, u_bank, sga, out_unit, with_bce, bslot=0):
            rhs = [xlt[:].rearrange("p (k t) -> p k t", k=KD)[:, k, :]
                   for k in range(KD)]
            g_tiles = wtiles(KD)
            mm_group(p[g_bank][:, 0:TT], rhs, g_tiles, ["xlt"], ("p", g_bank))
            u_tiles = wtiles(KD)
            mm_group(p[u_bank][:, 0:TT], rhs, u_tiles, ["xlt"], ("p", u_bank))
            ae_op(lambda a: a.activation(
                sgb[:, sga * TT:(sga + 1) * TT], p[g_bank][:, 0:TT], AF.Silu),
                reads=[("p", g_bank)], writes=[("sgb", sga)])
            if with_bce:
                ve_op(lambda v: v.tensor_tensor(
                    ev[:, sga * TT:(sga + 1) * TT],
                    sgb[:, sga * TT:(sga + 1) * TT],
                    p[u_bank][:, 0:TT], ALU.mult),
                    reads=[("sgb", sga), ("p", u_bank)], writes=[("ev", sga)])
                ve_op(lambda v: v.tensor_tensor(
                    big[:, out_unit * TT:(out_unit + 1) * TT],
                    ev[:, sga * TT:(sga + 1) * TT],
                    bce[:, 0:TT], ALU.mult),
                    reads=[("ev", sga), "bce"],
                    writes=[("big", out_unit)])
            else:
                ve_op(lambda v: v.tensor_tensor(
                    big[:, out_unit * TT:(out_unit + 1) * TT],
                    sgb[:, sga * TT:(sga + 1) * TT],
                    p[u_bank][:, 0:TT], ALU.mult),
                    reads=[("sgb", sga), ("p", u_bank)],
                    writes=[("big", out_unit)])

        def shared_pre(tau):
            base = NE * FH
            for f in range(0, min(2, FSH)):
                up_pair(f, f % 2, 2 + (f % 2), f % 2, base + f, False)

        def shared_post(tau):
            base = NE * FH
            for f in range(min(2, FSH), FSH):
                up_pair(f, f % 2, 2 + (f % 2), f % 2, base + f, False)

        def crow_dma(ee):
            ac_op("cr0", lambda s, ee=ee: s.dma_start(
                crow[0:1, 0:TT], cem[ee:ee + 1, 0:TT]),
                reads=["cem"], writes=["crow"])

        def bce_bcast(ee):
            cslot = ee % 2
            pe_op(lambda t, cslot=cslot: t.matmul(
                p[4 + cslot][:, 0:TT], ones1[:], crow[0:1, 0:TT],
                start=True, stop=True),
                reads=["crow", "ones1"], writes=[("p", 4 + cslot)])

        def bce_make(ee):
            crow_dma(ee)
            bce_bcast(ee)

        def bce_evac(ee):
            cslot = ee % 2
            ve_op(lambda v, cslot=cslot: v.tensor_copy(
                bce[:, 0:TT], p[4 + cslot][:, 0:TT]),
                reads=[("p", 4 + cslot)], writes=["bce"])

        def experts_up(tau):
            bce_make(0)
            for ee in range(NE):
                bce_evac(ee)
                for f in range(FH):
                    up_pair(f, f % 2, 2 + (f % 2), f % 2, ee * FH + f, True,
                            bslot=ee % 2)
                    if f == 0 and ee + 1 < NE:
                        crow_dma(ee + 1)
                    if f == min(2, FH - 1) and ee + 1 < NE:
                        bce_bcast(ee + 1)

        def ffn_down(tau, cur):
            akeys = [("big", i) for i in range(NU)]
            NJ = NE * FH + FSH
            PIECE = 2 * CT
            prevj = [None]
            for c in range(CO):
                bank = c % 2
                rhs = []
                for ee in range(NE):
                    for k in range(FH):
                        rhs.append(big[:, (ee * FH + k) * TT:
                                       (ee * FH + k + 1) * TT])
                for k in range(FSH):
                    rhs.append(big[:, (NE * FH + k) * TT:(NE * FH + k + 1) * TT])
                i0 = 0
                first = True
                while i0 < NJ:
                    nn_ = min(PIECE, NJ - i0)
                    tiles = wtiles(nn_)
                    mm_group(p[bank][:, 0:TT], rhs[i0:i0 + nn_], tiles,
                             (akeys if first else []), ("p", bank),
                             grp_start=first, grp_stop=(i0 + nn_ == NJ))
                    i0 += nn_
                    first = False
                if prevj[0] is not None:
                    residual_block(*prevj[0])
                prevj[0] = (c, bank, cur)
            residual_block(*prevj[0])

        pend0_seen.update([0, 1])
        emit_stage_cast(0)
        emit_stage_cast(1)

        for tau in range(NT):
            cur = tau % 2
            nxt = 1 - cur

            def src(pt, cur=cur):
                return xt[cur][:, pt * D:(pt + 1) * D]
            layernorm(src, PT, 0, 0, ("xt", cur), "ln", stati)
            if tau == 0:
                halo_ln()
            transpose_to_xlt("xlt")
            if tau + 1 < NT and tau > 1:
                ac_op("xd", mk_xload(nxt, tau + 1), writes=[("xt", nxt)])
            attn_up(tau)
            attn_down(tau, cur)
            layernorm(src, PT, 0, 0, ("xt", cur), "ln", stati,
                      from_partials=True)
            transpose_to_xlt("xlt")
            shared_pre(tau)
            router(tau)
            shared_post(tau)
            experts_up(tau)
            ffn_down(tau, cur)

            def mk_store(cur, tau):
                def f(s, cur=cur, tau=tau):
                    dst = out_h[tau * TT:(tau + 1) * TT, :].rearrange(
                        "(t p) d -> p t d", p=128)
                    return s.dma_start(dst, xt[cur][:].rearrange(
                        "p (t d) -> p t d", t=PT))
                return f
            ac_op("st", mk_store(cur, tau), reads=[("xt", cur)])
            if tau == 0 and NT > 1:
                ac_op("xd", mk_xload(1, 1),
                      writes=[("xt", 1)],
                      wars=[("stg", NCH - 2), ("stg", NCH - 1), "halo"])
            if tau == 1 and NT > 2:
                ac_op("xd", mk_xload(0, 2),
                      writes=[("xt", 0)],
                      wars=[("stg", 2 * NCH - 2), ("stg", 2 * NCH - 1)])

        assert wt[0] == cfg.n_pass * NT, (wt[0], cfg.n_pass * NT)

        S = sems

        def emit_waits(h, waits):
            for s, v in waits:
                h.wait_ge(S[s], v)

        @block.tensor
        def _(t):
            for waits, fn in em.ops["pe"]:
                emit_waits(t, waits)
                fn(t).then_inc(S["pe"], 1)

        @block.vector
        def _(v):
            for waits, fn in em.ops["ve"]:
                emit_waits(v, waits)
                fn(v).then_inc(S["ve"], 1)

        @block.scalar
        def _(a):
            for entry in em.ops["ae"]:
                if len(entry) == 3:
                    waits, fn, _ = entry
                    emit_waits(a, waits)
                    fn(a)
                else:
                    waits, fn = entry
                    emit_waits(a, waits)
                    fn(a).then_inc(S["ae"], 1)

        @block.gpsimd
        def _(g):
            g.memset(ident_b[:], 0.0).then_inc(S["po"], 1)
            g.wait_ge(S["po"], 1)
            g.affine_select(out=ident_b[:], in_=ident_b[:],
                            compare_op=ALU.not_equal, fill=1.0, base=0,
                            pattern=[[-1, 128]],
                            channel_multiplier=1).then_inc(S["po"], 1)
            g.memset(ident_f[:], 0.0).then_inc(S["po"], 1)
            g.wait_ge(S["po"], 3)
            g.affine_select(out=ident_f[:], in_=ident_f[:],
                            compare_op=ALU.not_equal, fill=1.0, base=0,
                            pattern=[[-1, 128]],
                            channel_multiplier=1).then_inc(S["po"], 1)
            g.memset(ones1[:], 1.0).then_inc(S["po"], 1)
            g.memset(icon[:], MAGIC).then_inc(S["po"], 1)
            for waits, fn in em.ops["po"]:
                emit_waits(g, waits)
                fn(g).then_inc(S["po"], 1)

        @block.sync
        def _(s):
            syhi = {}
            for waits, fn, sem, ser, lwars in sync_stream:
                allw = dict(waits)
                for k in lwars:
                    for s2, v2 in em.lastr.get(k, []):
                        allw[s2] = max(allw.get(s2, 0), v2)
                    if k in em.lastw:
                        s2, v2 = em.lastw[k]
                        allw[s2] = max(allw.get(s2, 0), v2)
                for s2, v2 in sorted(allw.items()):
                    if syhi.get(s2, 0) < v2:
                        s.wait_ge(S[s2], v2)
                        syhi[s2] = v2
                fn(s).then_inc(S[sem], 16)
                if ser is not None:
                    s.wait_ge(S[sem], ser)
            for sname, cnt_ in sorted(sy_cnt.items()):
                if sname != "st":
                    s.wait_ge(S[sname], 16 * cnt_)
            s.wait_ge(S["st"], 16 * st_dma[0])

    return nc


def make_host_arrays(cfg, w_in, w_out, conv_w, conv_b, Dvec, w_gate,
                     w_gate_up, w_down, w_sh_gate_up, w_sh_down):
    mats = {
        "w_inT": np.ascontiguousarray(np.asarray(w_in, np.float32).T),
        "w_outT": np.ascontiguousarray(np.asarray(w_out, np.float32).T),
        "w_sh_gate_up": np.asarray(w_sh_gate_up, np.float32),
        "w_sh_down": np.asarray(w_sh_down, np.float32),
    }
    for e in range(cfg.NE):
        mats["w_gate_up_%d" % e] = np.asarray(w_gate_up, np.float32)[e]
        mats["w_down_%d" % e] = np.asarray(w_down, np.float32)[e]
    ws = pack_wstream(cfg, mats)
    cwt = np.asarray(conv_w, np.float32)[:, 0, :]
    cw = np.ascontiguousarray(
        cwt.reshape(cfg.FXI, 128, 4).transpose(1, 0, 2).reshape(128, cfg.FXI * 4))
    cb = np.ascontiguousarray(
        np.asarray(conv_b, np.float32).reshape(cfg.FXI, 128).T)
    dd = np.ascontiguousarray(
        np.asarray(Dvec, np.float32).reshape(cfg.FXI, 128).T)
    wg = np.ascontiguousarray(
        np.asarray(w_gate, np.float32).reshape(cfg.KD, 128, cfg.NE)
        .transpose(1, 0, 2).reshape(128, cfg.KD * cfg.NE))
    return ws, cw, cb, dd, wg


def shard_x(cfg, x, b, t0):
    Dm = x.shape[-1]
    if t0 == 0:
        return np.ascontiguousarray(
            np.concatenate([np.zeros((3, Dm), np.float32),
                            x[b, 0:cfg.TC]], axis=0))
    return np.ascontiguousarray(x[b, t0 - 3: t0 + cfg.TC])


def kernel(L0_attn_x, L0_attn_w_in, L0_attn_w_out, L0_attn_conv_w,
           L0_attn_conv_b, L0_attn_D, L0_ffn_w_gate, L0_ffn_w_gate_up,
           L0_ffn_w_down, L0_ffn_w_sh_gate_up, L0_ffn_w_sh_down):
    cfg = REAL
    x = np.asarray(L0_attn_x, dtype=np.float32)
    B, T, Dm = x.shape
    n_cores = 8
    per_b = n_cores // B
    ws, cw, cb, dd, wg = make_host_arrays(
        cfg, L0_attn_w_in, L0_attn_w_out, L0_attn_conv_w, L0_attn_conv_b,
        L0_attn_D, L0_ffn_w_gate, L0_ffn_w_gate_up, L0_ffn_w_down,
        L0_ffn_w_sh_gate_up, L0_ffn_w_sh_down)
    in_maps = []
    for c in range(n_cores):
        b, q = divmod(c, per_b)
        in_maps.append({"x": shard_x(cfg, x, b, q * cfg.TC), "ws": ws,
                        "cw": cw, "cb": cb, "dd": dd, "wg": wg})
    nc = build_program(cfg)
    res = run_bass_kernel_spmd(nc, in_maps, core_ids=list(range(n_cores)))
    out = np.empty((B, T, Dm), np.float32)
    for c in range(n_cores):
        b, q = divmod(c, per_b)
        out[b, q * cfg.TC:(q + 1) * cfg.TC] = res.results[c]["out"]
    return out
